# version 06 (final)
# baseline (speedup 1.0000x reference)
import os
import numpy as np
import concourse.bass as bass
import concourse.mybir as mybir
from concourse.bass_utils import run_bass_kernel_spmd
from contextlib import ExitStack

F32 = mybir.dt.float32
BF16 = mybir.dt.bfloat16
F16 = mybir.dt.float16
U8 = mybir.dt.uint8
AF = mybir.ActivationFunctionType
OP = mybir.AluOpType

HID, PROJ, INTER, NH, NG, CS, DFF = 4096, 18560, 8192, 128, 8, 256, 11008
EPS = 1e-5
TPC = 2048
NCH = TPC // CS
TP4 = TPC + 4


class Prog:
    def __init__(self, nc, blk, E):
        self.blk = blk
        self.sB = E(nc.semaphore("sB"))
        self.sDG = E(nc.semaphore("sDG"))
        self.sDS = E(nc.semaphore("sDS"))
        self.nB = self.nDG = self.nDS = 0

    def dma(self, eng, out, in_, gp):
        i = eng.dma_start(out=out, in_=in_)
        if gp:
            self.nDG += 16
            i.then_inc(self.sDG, 16)
        else:
            self.nDS += 16
            i.then_inc(self.sDS, 16)
        return i

    def stage(self, **parts):
        pB, pG, pS = self.nB, self.nDG, self.nDS
        for name in ("gpsimd", "sync", "tensor", "vector", "scalar"):
            fn = parts.get(name)
            if fn is None:
                continue
            me = self

            def body(eng, fn=fn, name=name, pB=pB, pG=pG, pS=pS):
                eng.wait_ge(me.sB, pB)
                eng.wait_ge(me.sDG, pG)
                eng.wait_ge(me.sDS, pS)
                last = fn(eng)
                if name in ("tensor", "vector", "scalar"):
                    assert last is not None, f"{name} stage returned None"
                    last.then_inc(me.sB, 1)
                    me.nB += 1
            getattr(self.blk, name)(body)


def build_nc(mode):
    nc = bass.Bass("TRN2", target_bir_lowering=False)
    dIn = lambda n, s, d: nc.dram_tensor(n, s, d, kind="ExternalInput").ap()
    dOut = lambda n, s, d: nc.dram_tensor(n, s, d, kind="ExternalOutput").ap()
    _dbg = set((os.environ.get("KDBG") or "").split(","))

    def dTmp(n, s, d):
        if n in _dbg:
            return nc.dram_tensor(n, s, d, kind="ExternalOutput").ap()
        return nc.dram_tensor(n, s, d).ap()
    L2 = mode != "L1"

    xT = dIn("xT", [HID, TP4], F32)
    w_inT = dIn("w_inT", [HID, PROJ], F32)
    convw = dIn("convw", [128, 320], F32)
    convb = dIn("convb", [128, 80], F32)
    dtb_col = dIn("dtb_col", [128, 1], F32)
    alog_col = dIn("alog_col", [128, 1], F32)
    onesr = dIn("onesr", [1, 128], F32)
    onescol = dIn("onescol", [128, 1], F32)
    identf = dIn("identf", [128, 128], F32)
    if L2:
        x_res = dIn("x_res", [TPC, HID], F32)
        out_wT = dIn("out_wT", [INTER, HID], F32)
        d_bc = dIn("d_bc", [128, 128], F32)
        normw = dIn("normw", [128, 64], F32)
        masku_in = dIn("masku", [128, 2 * CS], U8)
        Lstack = dIn("Lstack", [8 * 128, INTER], F32)
        Dflat = dIn("Dflat", [1, 8 * NH], F32)
        mselp = dIn("mselp", [1, 8], F32)
        mpd_m = dIn("mpd_m", [1, 64], F32)
        mpd_1m = dIn("mpd_1m", [1, 64], F32)
        qg = dIn("qg", [2048, DFF], U8)
        qu = dIn("qu", [2048, DFF], U8)
        qd = dIn("qd", [5504, HID], U8)
        sgb = dIn("sgb", [HID, DFF], F16)
        sub_ = dIn("sub_", [HID, DFF], F16)
        sdb = dIn("sdb", [DFF, HID], F16)
        zgb = dIn("zgb", [HID, DFF // 2], U8)
        zub = dIn("zub", [HID, DFF // 2], U8)
        zdb = dIn("zdb", [DFF, HID // 2], U8)
        y_out = dOut("y_out", [TPC, HID], F32)
    else:
        L_out = dOut("L_out", [128, INTER], F32)
        D_out = dOut("D_out", [1, NH], F32)

    projT_s = dTmp("projT_s", [PROJ, TP4], BF16)
    if L2:
        xbca_s = dTmp("xbca_s", [128, NCH * 80 * CS], BF16)
        hstok_s = dTmp("hstok_s", [128, NCH * 2 * INTER], BF16)
        btok_s = dTmp("btok_s", [128, NCH * 2048], BF16)
        gh_s = dTmp("gh_s", [INTER, TPC], BF16)
        ao_s = dTmp("ao_s", [TPC, HID], F16)
        x2_s = dTmp("x2_s", [TPC, HID], F32)
        x2T_s = dTmp("x2T_s", [HID, TPC], BF16)
        interT_s = dTmp("interT_s", [DFF, TPC], BF16)
        wg_s = dTmp("wg_s", [HID, DFF], BF16)
        wu_s = dTmp("wu_s", [HID, DFF], BF16)
        wd_s = dTmp("wd_s", [DFF, HID], BF16)

    with ExitStack() as st:
        E = st.enter_context
        blk = E(nc.Block())
        P = Prog(nc, blk, E)

        ident_bf = E(nc.sbuf_tensor("ident_bf", [128, 128], BF16))
        ident_f = E(nc.sbuf_tensor("ident_f", [128, 128], F32))
        ident_h = E(nc.sbuf_tensor("ident_h", [128, 128], F16))
        ones1f = E(nc.sbuf_tensor("ones1f", [1, 128], F32))
        onescol_b = E(nc.sbuf_tensor("onescol_b", [128, 1], BF16))
        convw_t = E(nc.sbuf_tensor("convw_t", [128, 320], F32))
        convb_t = E(nc.sbuf_tensor("convb_t", [128, 80], F32))
        dtb_t = E(nc.sbuf_tensor("dtb_t", [128, 1], F32))
        acol_t = E(nc.sbuf_tensor("acol_t", [128, 1], F32))
        constp = E(nc.sbuf_tensor("constp", [128, 3], F32))
        if L2:
            drep_t = E(nc.sbuf_tensor("drep_t", [128, 128], F32))
            normw_t = E(nc.sbuf_tensor("normw_t", [128, 64], F32))
            masku_t = E(nc.sbuf_tensor("masku_t", [128, 2 * CS], U8))

        psA = E(nc.psum_tensor("psA", [128, 1024], F32))
        psB = E(nc.psum_tensor("psB", [128, 1024], F32))
        pb4 = E(nc.psum_tensor("pb4", [128, 512], F32))
        pb5 = E(nc.psum_tensor("pb5", [128, 512], F32))
        pbt = E(nc.psum_tensor("pbt", [128, 1024], BF16))
        pbt2 = E(nc.psum_tensor("pbt2", [128, 1024], F16))
        SLOT6 = [psA[:, 0:512], psA[:, 512:1024], psB[:, 0:512],
                 psB[:, 512:1024], pb4[:, 0:512], pb5[:, 0:512]]

        def _p0(g):
            P.dma(g, ident_bf[:], identf, True)
            P.dma(g, ident_f[:], identf, True)
            P.dma(g, ident_h[:], identf, True)
            P.dma(g, ones1f[:], onesr, True)
            P.dma(g, onescol_b[:], onescol, True)
            P.dma(g, convw_t[:], convw, True)
            P.dma(g, convb_t[:], convb, True)
            P.dma(g, dtb_t[:], dtb_col, True)
            P.dma(g, acol_t[:], alog_col, True)
            if L2:
                P.dma(g, drep_t[:], d_bc, True)
                P.dma(g, normw_t[:], normw, True)
                P.dma(g, masku_t[:], masku_in, True)
        P.stage(gpsimd=_p0)
        P.stage(scalar=lambda s: s.activation(acol_t[:], acol_t[:], AF.Exp))
        P.stage(vector=lambda v: v.tensor_scalar(acol_t[:], acol_t[:], -1.0, None, OP.mult))

        def _cst(v):
            v.memset(constp[:, 0:1], EPS)
            v.memset(constp[:, 1:2], 1.0 / 1024.0)
            return v.memset(constp[:, 2:3], -0.5)
        P.stage(vector=_cst)

        with ExitStack() as ph:
            Ep = ph.enter_context
            xTs = Ep(nc.sbuf_tensor("xTs", [128, 32 * TP4], BF16))
            wsl = [Ep(nc.sbuf_tensor(f"wsl{i}", [128, HID], BF16)) for i in range(2)]
            pstg = Ep(nc.sbuf_tensor("pstg", [128, TP4], BF16))

            def _ldx(g):
                for k in range(32):
                    src = bass.AP(xT.tensor, k * 128 * TP4, [[TP4, 128], [1, TP4]])
                    P.dma(g, xTs[:, k * TP4:(k + 1) * TP4], src, True)
            P.stage(gpsimd=_ldx)

            OT0, OT1 = (64, 145) if mode == "L1" else (0, 145)
            TCH = [(0, 512), (512, 512), (1024, 512), (1536, 512), (2048, 4)]

            def _ldw(g, o):
                src = bass.AP(w_inT.tensor, o * 128,
                              [[PROJ, 128], [128 * PROJ, 32], [1, 128]])
                dst = bass.AP(wsl[o % 2], 0, [[HID, 128], [128, 32], [1, 128]])
                P.dma(g, dst, src, True)

            P.stage(gpsimd=lambda g: _ldw(g, OT0))
            for o in range(OT0, OT1):
                def _mm(t, o=o):
                    wb = wsl[o % 2]
                    for ci, (t0, tn) in enumerate(TCH):
                        for k in range(32):
                            i = t.matmul(SLOT6[ci][:, 0:tn], wb[:, k * 128:(k + 1) * 128],
                                         xTs[:, k * TP4 + t0:k * TP4 + t0 + tn],
                                         start=(k == 0), stop=(k == 31))
                    return i

                def _pre(g, o=o):
                    if o + 1 < OT1:
                        _ldw(g, o + 1)
                P.stage(tensor=_mm, gpsimd=_pre)

                def _ev(v):
                    for ci, (t0, tn) in enumerate(TCH):
                        i = v.tensor_copy(pstg[:, t0:t0 + tn], SLOT6[ci][:, 0:tn])
                    return i
                P.stage(vector=_ev)
                P.stage(sync=lambda sy, o=o: P.dma(
                    sy, projT_s[o * 128:(o + 1) * 128, :], pstg[:], False))

        with ExitStack() as ph:
            Ep = ph.enter_context
            S_t = Ep(nc.sbuf_tensor("S_t", [128, INTER], F32))
            dacc_t = Ep(nc.sbuf_tensor("dacc_t", [128, 1], F32))
            def _init(v):
                v.memset(S_t[:], 0.0)
                return v.memset(dacc_t[:], 1.0)
            P.stage(vector=_init)
            if L2:
                cstk = ExitStack()
                Ec = cstk.enter_context
                dfl = Ec(nc.sbuf_tensor("dfl", [1, 1024], F32))
                msl = Ec(nc.sbuf_tensor("msl", [1, 8], F32))
                mmt = Ec(nc.sbuf_tensor("mmt", [1, 64], F32))
                m1m = Ec(nc.sbuf_tensor("m1m", [1, 64], F32))
                coefr = Ec(nc.sbuf_tensor("coefr", [1, 128], F32))
                tmpr = Ec(nc.sbuf_tensor("tmpr", [1, 128], F32))
                crep = Ec(nc.sbuf_tensor("crep", [128, 128], F32))
                Ljt = Ec(nc.sbuf_tensor("Ljt", [128, INTER], F32))

                def _ldc(g):
                    P.dma(g, dfl[:], Dflat, True)
                    P.dma(g, msl[:], mselp, True)
                    P.dma(g, mmt[:], mpd_m, True)
                    P.dma(g, m1m[:], mpd_1m, True)
                P.stage(gpsimd=_ldc)
                for j in range(8):
                    P.stage(vector=lambda v: v.memset(coefr[:], 1.0),
                            gpsimd=lambda g, j=j: P.dma(
                                g, Ljt[:], Lstack[j * 128:(j + 1) * 128, :], True))
                    for k in range(8):
                        P.stage(vector=lambda v, j=j, k=k: v.tensor_scalar(
                            tmpr[:], dfl[0:1, k * 128:(k + 1) * 128],
                            mmt[0:1, 8 * j + k:8 * j + k + 1],
                            m1m[0:1, 8 * j + k:8 * j + k + 1], OP.mult, OP.add))
                        P.stage(vector=lambda v: v.tensor_mul(coefr[:], coefr[:], tmpr[:]))
                    P.stage(vector=lambda v, j=j: v.tensor_scalar(
                        coefr[:], coefr[:], msl[0:1, j:j + 1], None, OP.mult))
                    P.stage(tensor=lambda t: t.matmul(
                        psA[:, 0:128], ones1f[:], coefr[:], start=True, stop=True))
                    P.stage(vector=lambda v: v.tensor_copy(crep[:], psA[:, 0:128]))

                    def _acc(v):
                        for h in range(NH):
                            i = v.scalar_tensor_tensor(
                                S_t[:, h * 64:(h + 1) * 64], Ljt[:, h * 64:(h + 1) * 64],
                                crep[:, h:h + 1], S_t[:, h * 64:(h + 1) * 64],
                                OP.mult, OP.add)
                        return i
                    P.stage(vector=_acc)
                cstk.close()
            convin = Ep(nc.sbuf_tensor("convin", [128, 40 * 260], BF16))
            cacc = Ep(nc.sbuf_tensor("cacc", [128, 10 * 256], F32))
            xbca = Ep(nc.sbuf_tensor("xbca", [128, 80 * 256], BF16))
            hstok = Ep(nc.sbuf_tensor("hstok", [128, 2 * INTER], BF16))
            btok = Ep(nc.sbuf_tensor("btok", [128, 2048], BF16))
            dtc = Ep(nc.sbuf_tensor("dtc", [128, 256], BF16))
            zf = Ep(nc.sbuf_tensor("zf", [128, 256], F32))
            az = Ep(nc.sbuf_tensor("az", [128, 256], F32))
            eaz = Ep(nc.sbuf_tensor("eaz", [128, 256], F32))
            lse = Ep(nc.sbuf_tensor("lse", [128, 256], F32))
            rl = Ep(nc.sbuf_tensor("rl", [128, 256], F32))
            dtv = Ep(nc.sbuf_tensor("dtv", [128, 256], F32))
            dA = Ep(nc.sbuf_tensor("dA", [128, 256], F32))
            cum = Ep(nc.sbuf_tensor("cum", [128, 256], F32))
            z256 = Ep(nc.sbuf_tensor("z256", [128, 256], F32))
            lnd = Ep(nc.sbuf_tensor("lnd", [128, 256], F32))
            biasm = Ep(nc.sbuf_tensor("biasm", [128, 256], F32))
            ecl = Ep(nc.sbuf_tensor("ecl", [128, 256], F32))
            wdt = Ep(nc.sbuf_tensor("wdt", [128, 256], F32))
            cdec = Ep(nc.sbuf_tensor("cdec", [128, 1], F32))
            biasT = Ep(nc.sbuf_tensor("biasT", [128, 256], F32))
            wdtT = Ep(nc.sbuf_tensor("wdtT", [128, 256], F32))
            cdrep = Ep(nc.sbuf_tensor("cdrep", [128, 128], F32))
            cdflat = Ep(nc.sbuf_tensor("cdflat", [1, 128], F32))
            cumflat = Ep(nc.sbuf_tensor("cumflat", [1, 16 * 256], F32))
            xw = Ep(nc.sbuf_tensor("xw", [128, 2 * 16 * 64], BF16))
            sbf = Ep(nc.sbuf_tensor("sbf", [128, 1024], BF16))
            if L2:
                gatec = Ep(nc.sbuf_tensor("gatec", [128, 8 * 256], BF16))
                cib = Ep(nc.sbuf_tensor("cib", [128, 256], F32))
                e2b = Ep(nc.sbuf_tensor("e2b", [128, 256], F32))
                etb = Ep(nc.sbuf_tensor("etb", [128, 2 * 256], BF16))
                emb = Ep(nc.sbuf_tensor("emb", [128, 2 * 256], BF16))
                scf = Ep(nc.sbuf_tensor("scf", [128, 2 * 256], BF16))
                zb256 = Ep(nc.sbuf_tensor("zb256", [128, 2 * 256], BF16))
                t1b = Ep(nc.sbuf_tensor("t1b", [128, 256], F32))
                t2b = Ep(nc.sbuf_tensor("t2b", [128, 256], F32))
                ygrp = Ep(nc.sbuf_tensor("ygrp", [128, 8 * 256], BF16))
                sgate = Ep(nc.sbuf_tensor("sgate", [128, 8 * 256], BF16))
                ggrp = Ep(nc.sbuf_tensor("ggrp", [128, 8 * 256], BF16))
                rst1 = Ep(nc.sbuf_tensor("rst1", [1, 256], F32))
                rst2 = Ep(nc.sbuf_tensor("rst2", [1, 256], F32))

            def _z0(v):
                if L2:
                    v.memset(zb256[:], 0.0)
                return v.memset(z256[:], 0.0)
            P.stage(vector=_z0)

            for c in range(NCH):
                co = 4 + c * CS

                def _lddt(g, co=co):
                    src = bass.AP(projT_s.tensor, (144 * 128) * TP4 + co,
                                  [[TP4, 128], [1, 256]])
                    P.dma(g, dtc[:], src, True)
                P.stage(gpsimd=_lddt)

                for hf in range(2):
                    def _ldch(g, co=co, hf=hf):
                        src = bass.AP(projT_s.tensor, ((64 + hf * 40) * 128) * TP4 + co - 3,
                                      [[TP4, 128], [128 * TP4, 40], [1, 259]])
                        dst = bass.AP(convin, 0, [[260 * 40, 128], [260, 40], [1, 259]])
                        P.dma(g, dst, src, True)
                    P.stage(gpsimd=_ldch)
                    for qh in range(4):
                        def _cv(v, hf=hf, qh=qh):
                            for ti in range(10):
                                lt = qh * 10 + ti
                                tt = hf * 40 + lt
                                for k in range(4):
                                    s0 = convin[:, lt * 260 + k:lt * 260 + k + 256]
                                    wk = convw_t[:, tt * 4 + k:tt * 4 + k + 1]
                                    if k == 0:
                                        i = v.tensor_scalar(cacc[:, ti * 256:(ti + 1) * 256],
                                                            s0, wk, None, OP.mult)
                                    else:
                                        i = v.scalar_tensor_tensor(
                                            cacc[:, ti * 256:(ti + 1) * 256], s0, wk,
                                            cacc[:, ti * 256:(ti + 1) * 256], OP.mult, OP.add)
                            return i
                        P.stage(vector=_cv)

                        def _sl(s, hf=hf, qh=qh):
                            for ti in range(10):
                                tt = hf * 40 + qh * 10 + ti
                                i = s.activation(xbca[:, tt * 256:(tt + 1) * 256],
                                                 cacc[:, ti * 256:(ti + 1) * 256], AF.Silu,
                                                 bias=convb_t[:, tt:tt + 1])
                            return i
                        P.stage(scalar=_sl)

                blocks = [(ct, jt) for ct in range(64) for jt in range(2)]
                bblocks = [(64 + gt, jt) for gt in range(8) for jt in range(2)]
                for r in range(18):
                    grp = (blocks + bblocks)[r * 8:(r + 1) * 8]

                    def _tr(t, grp=grp):
                        for u, (ct, jt) in enumerate(grp):
                            i = t.transpose(pbt[:, u * 128:(u + 1) * 128],
                                            xbca[:, ct * 256 + jt * 128:ct * 256 + jt * 128 + 128],
                                            ident_bf[:])
                        return i
                    P.stage(tensor=_tr)

                    def _te(v, grp=grp):
                        for u, (ct, jt) in enumerate(grp):
                            if ct < 64:
                                dst = hstok[:, jt * INTER + ct * 128:jt * INTER + ct * 128 + 128]
                            else:
                                gt = ct - 64
                                dst = btok[:, jt * 1024 + gt * 128:jt * 1024 + gt * 128 + 128]
                            i = v.tensor_copy(dst, pbt[:, u * 128:(u + 1) * 128])
                        return i
                    P.stage(vector=_te)

                P.stage(vector=lambda v: v.tensor_scalar(zf[:], dtc[:], dtb_t[:, 0:1], None, OP.add))
                P.stage(scalar=lambda s: s.activation(az[:], zf[:], AF.Abs))
                P.stage(scalar=lambda s: s.activation(eaz[:], az[:], AF.Exp, scale=-1.0))
                P.stage(scalar=lambda s: s.activation(lse[:], eaz[:], AF.Ln, bias=1.0))
                P.stage(vector=lambda v: v.tensor_scalar(rl[:], zf[:], 0.0, None, OP.max))
                P.stage(vector=lambda v: v.tensor_add(dtv[:], rl[:], lse[:]))
                P.stage(vector=lambda v: v.tensor_scalar(dA[:], dtv[:], acol_t[:, 0:1], None, OP.mult))
                P.stage(vector=lambda v: v.tensor_tensor_scan(cum[:], dA[:], z256[:], 0.0, OP.add, OP.add))
                P.stage(scalar=lambda s: s.activation(cdec[:], cum[:, 255:256], AF.Exp))
                P.stage(scalar=lambda s: s.activation(ecl[:], cum[:], AF.Exp, scale=-1.0,
                                                      bias=cum[:, 255:256]))
                P.stage(vector=lambda v: v.tensor_mul(wdt[:], dtv[:], ecl[:]))
                P.stage(vector=lambda v: v.tensor_mul(dacc_t[:], dacc_t[:], cdec[:]))
                if L2:
                    P.stage(scalar=lambda s: s.activation(lnd[:], dtv[:], AF.Ln))
                    P.stage(vector=lambda v: v.tensor_sub(biasm[:], lnd[:], cum[:]))

                def _fl(g):
                    P.dma(g, cdflat[0:1, :], cdec[:, 0:1], True)
                P.stage(gpsimd=_fl)

                def _trf(t):
                    for jt in range(2):
                        t.transpose(psA[:, jt * 512:jt * 512 + 128],
                                    wdt[:, jt * 128:(jt + 1) * 128], ident_f[:])
                    if L2:
                        for jt in range(2):
                            t.transpose(psB[:, jt * 512:jt * 512 + 128],
                                        biasm[:, jt * 128:(jt + 1) * 128], ident_f[:])
                    return t.matmul(pb4[:, 0:128], ones1f[:], cdflat[:], start=True, stop=True)
                P.stage(tensor=_trf)

                def _evf(v):
                    for jt in range(2):
                        v.tensor_copy(wdtT[:, jt * 128:(jt + 1) * 128],
                                      psA[:, jt * 512:jt * 512 + 128])
                        if L2:
                            v.tensor_copy(biasT[:, jt * 128:(jt + 1) * 128],
                                          psB[:, jt * 512:jt * 512 + 128])
                    return v.tensor_copy(cdrep[:], pb4[:, 0:128])
                P.stage(vector=_evf)

                for gidx in range(NG):
                    gc0 = gidx * 1024

                    def _sbfc(v, gc0=gc0):
                        return v.tensor_copy(sbf[:], S_t[:, gc0:gc0 + 1024])
                    P.stage(vector=_sbfc)

                    if L2:
                        def _ldgrp(g, gidx=gidx):
                            P.dma(g, cumflat[0:1, :], cum[gidx * 16:(gidx + 1) * 16, :], True)
                            src2 = bass.AP(projT_s.tensor, co + (gidx * 8) * 128 * TP4,
                                           [[256, 8], [256 * 8, 128], [1, 256]])
                            pass
                        def _ldg2(g, gidx=gidx, co=co):
                            P.dma(g, cumflat[0:1, :], cum[gidx * 16:(gidx + 1) * 16, :], True)
                            sg_src = bass.AP(projT_s.tensor, (gidx * 8 * 128) * TP4 + co,
                                             [[TP4, 128], [128 * TP4, 8], [1, 256]])
                            sg_dst = bass.AP(gatec, 0, [[256 * 8, 128], [256, 8], [1, 256]])
                            P.dma(g, sg_dst, sg_src, True)
                        P.stage(gpsimd=_ldg2)

                        def _scm(t, gidx=gidx):
                            for jt in range(2):
                                i = t.matmul(psB[:, jt * 512:jt * 512 + 256],
                                             xbca[:, (64 + gidx) * 256 + jt * 128:
                                                  (64 + gidx) * 256 + jt * 128 + 128],
                                             xbca[:, (72 + gidx) * 256:(72 + gidx) * 256 + 256],
                                             start=True, stop=True)
                            return i
                        P.stage(tensor=_scm)

                        for hh in range(16):
                            h = gidx * 16 + hh
                            P.stage(tensor=lambda t, hh=hh: t.matmul(
                                pb4[:, 0:256], ones1f[:],
                                cumflat[0:1, hh * 256:(hh + 1) * 256], start=True, stop=True))
                            P.stage(vector=lambda v: v.tensor_copy(cib[:], pb4[:, 0:256]))

                            def _eact(s, h=h):
                                for jt in range(2):
                                    s.activation(etb[:, jt * 256:(jt + 1) * 256], cib[:],
                                                 AF.Exp, bias=biasT[:, jt * 128 + h:jt * 128 + h + 1])
                                return s.activation(e2b[:], cib[:], AF.Exp)
                            P.stage(scalar=_eact)

                            def _em(v):
                                for jt in range(2):
                                    i = v.select(emb[:, jt * 256:(jt + 1) * 256],
                                                 masku_t[:, jt * 256:(jt + 1) * 256],
                                                 etb[:, jt * 256:(jt + 1) * 256],
                                                 zb256[:, jt * 256:(jt + 1) * 256])
                                return i
                            P.stage(vector=_em)

                            def _sf(v):
                                for jt in range(2):
                                    i = v.tensor_mul(scf[:, jt * 256:(jt + 1) * 256],
                                                     emb[:, jt * 256:(jt + 1) * 256],
                                                     psB[:, jt * 512:jt * 512 + 256])
                                return i
                            P.stage(vector=_sf)

                            hp = 64 * (hh % 2)

                            def _ymm(t, h=h, hh=hh, gidx=gidx, hp=hp):
                                for jt in range(2):
                                    t.matmul(psA[hp:hp + 64, 0:256],
                                             hstok[:, jt * INTER + h * 64:jt * INTER + h * 64 + 64],
                                             scf[:, jt * 256:(jt + 1) * 256],
                                             start=(jt == 0), stop=(jt == 1),
                                             tile_position=(0, hp))
                                return t.matmul(psA[hp:hp + 64, 256:512],
                                                sbf[:, hh * 64:(hh + 1) * 64],
                                                xbca[:, (72 + gidx) * 256:(72 + gidx) * 256 + 256],
                                                start=True, stop=True,
                                                tile_position=(0, hp))
                            P.stage(tensor=_ymm)

                            P.stage(vector=lambda v, hp=hp: v.tensor_mul(
                                t1b[hp:hp + 64, :], e2b[hp:hp + 64, :], psA[hp:hp + 64, 256:512]))
                            P.stage(vector=lambda v, hp=hp: v.tensor_add(
                                t2b[hp:hp + 64, :], t1b[hp:hp + 64, :], psA[hp:hp + 64, 0:256]))

                            def _yf(v, h=h, hh=hh, hp=hp):
                                hs_h = xbca[:, (h // 2) * 256:(h // 2) * 256 + 256][hp:hp + 64, :]
                                return v.scalar_tensor_tensor(
                                    ygrp[hp:hp + 64, (hh // 2) * 256:(hh // 2) * 256 + 256],
                                    hs_h, drep_t[hp:hp + 64, h:h + 1],
                                    t2b[hp:hp + 64, :], OP.mult, OP.add)
                            P.stage(vector=_yf)

                    def _xw(v, gidx=gidx):
                        for jt in range(2):
                            for hh in range(16):
                                h = gidx * 16 + hh
                                i = v.tensor_scalar(
                                    xw[:, (jt * 16 + hh) * 64:(jt * 16 + hh) * 64 + 64],
                                    hstok[:, jt * INTER + h * 64:jt * INTER + h * 64 + 64],
                                    wdtT[:, jt * 128 + h:jt * 128 + h + 1], None, OP.mult)
                        return i
                    P.stage(vector=_xw)

                    def _stm(t, gidx=gidx):
                        for hh in range(16):
                            for jt in range(2):
                                i = t.matmul(psB[:, hh * 64:(hh + 1) * 64],
                                             btok[:, jt * 1024 + gidx * 128:jt * 1024 + gidx * 128 + 128],
                                             xw[:, (jt * 16 + hh) * 64:(jt * 16 + hh) * 64 + 64],
                                             start=(jt == 0), stop=(jt == 1))
                        return i
                    P.stage(tensor=_stm)

                    def _sup(v, gc0=gc0):
                        for hh in range(16):
                            i = v.scalar_tensor_tensor(
                                S_t[:, gc0 + hh * 64:gc0 + (hh + 1) * 64],
                                S_t[:, gc0 + hh * 64:gc0 + (hh + 1) * 64],
                                cdrep[:, (gc0 // 64 + hh) % 128:(gc0 // 64 + hh) % 128 + 1],
                                psB[:, hh * 64:(hh + 1) * 64], OP.mult, OP.add)
                        return i
                    P.stage(vector=_sup)

                    if L2:
                        def _sg(s, gidx=gidx):
                            for ti in range(8):
                                i = s.activation(sgate[:, ti * 256:(ti + 1) * 256],
                                                 gatec[:, ti * 256:(ti + 1) * 256], AF.Silu)
                            return i
                        P.stage(scalar=_sg)

                        def _gm(v):
                            for ti in range(8):
                                i = v.tensor_mul(ggrp[:, ti * 256:(ti + 1) * 256],
                                                 ygrp[:, ti * 256:(ti + 1) * 256],
                                                 sgate[:, ti * 256:(ti + 1) * 256])
                            return i
                        P.stage(vector=_gm)

                        def _gq(s):
                            for ti in range(8):
                                i = s.activation(sgate[:, ti * 256:(ti + 1) * 256],
                                                 ggrp[:, ti * 256:(ti + 1) * 256], AF.Square)
                            return i
                        P.stage(scalar=_gq)

                        def _vm(t):
                            for ti in range(8):
                                i = t.matmul(pb4[0:1, 0:256], onescol_b[:],
                                             sgate[:, ti * 256:(ti + 1) * 256],
                                             start=(ti == 0), stop=(ti == 7))
                            return i
                        P.stage(tensor=_vm)
                        P.stage(scalar=lambda s: s.activation(rst1[:], pb4[0:1, 0:256],
                                                              AF.Ln, scale=constp[0:1, 1:2], bias=constp[0:1, 0:1]))
                        P.stage(scalar=lambda s: s.activation(rst2[:], rst1[:], AF.Exp, scale=constp[0:1, 2:3]))
                        P.stage(tensor=lambda t: t.matmul(pb5[:, 0:256], ones1f[:], rst2[:],
                                                          start=True, stop=True))

                        def _gn(v, gidx=gidx):
                            for ti in range(8):
                                ct = gidx * 8 + ti
                                i = v.scalar_tensor_tensor(
                                    ygrp[:, ti * 256:(ti + 1) * 256],
                                    ggrp[:, ti * 256:(ti + 1) * 256],
                                    normw_t[:, ct:ct + 1], pb5[:, 0:256], OP.mult, OP.mult)
                            return i
                        P.stage(vector=_gn)

                        def _gst(sy, c=c, gidx=gidx):
                            dst = bass.AP(gh_s.tensor, (gidx * 1024) * TPC + c * 256,
                                          [[TPC, 128], [TPC * 128, 8], [1, 256]])
                            src = bass.AP(ygrp, 0, [[2048, 128], [256, 8], [1, 256]])
                            P.dma(sy, dst, src, False)
                        P.stage(sync=_gst)

            if mode == "L1":
                P.stage(sync=lambda sy: P.dma(sy, L_out, S_t[:], False))
                P.stage(gpsimd=lambda g: P.dma(g, cdflat[0:1, :], dacc_t[:, 0:1], True))
                P.stage(sync=lambda sy: P.dma(sy, D_out, cdflat[0:1, :], False))

        if L2:
            _build_back_half(nc, P, locals())

        def _fin(sy):
            sy.wait_ge(P.sB, P.nB)
            sy.wait_ge(P.sDG, P.nDG)
            sy.wait_ge(P.sDS, P.nDS)
        blk.sync(_fin)
    return nc


def _build_back_half(nc, P, env):
    g_ = env
    blk = g_["P"].blk
    E_ = None
    for k in ("psA", "psB", "pb4", "pb5", "pbt", "pbt2", "ident_bf", "ident_h",
              "ones1f", "onescol_b", "out_wT", "x_res", "ao_s", "x2_s", "x2T_s",
              "gh_s", "interT_s", "wg_s", "wu_s", "wd_s", "qg", "qu", "qd",
              "sgb", "sub_", "sdb", "zgb", "zub", "zdb", "y_out"):
        globals()["_T_" + k] = g_[k]
    psA, psB, pb4, pb5, pbt, pbt2 = g_["psA"], g_["psB"], g_["pb4"], g_["pb5"], g_["pbt"], g_["pbt2"]
    ident_bf, ident_h = g_["ident_bf"], g_["ident_h"]
    ones1f, onescol_b = g_["ones1f"], g_["onescol_b"]
    out_wT, x_res, ao_s, x2_s, x2T_s = g_["out_wT"], g_["x_res"], g_["ao_s"], g_["x2_s"], g_["x2T_s"]
    gh_s, interT_s = g_["gh_s"], g_["interT_s"]
    constp = g_["constp"]
    wg_s, wu_s, wd_s = g_["wg_s"], g_["wu_s"], g_["wd_s"]
    qg, qu, qd = g_["qg"], g_["qu"], g_["qd"]
    sgb, sub_, sdb = g_["sgb"], g_["sub_"], g_["sdb"]
    zgb, zub, zdb = g_["zgb"], g_["zub"], g_["zdb"]
    y_out = g_["y_out"]
    SLOT6 = g_["SLOT6"]

    with ExitStack() as ph:
        Ep = ph.enter_context
        ghh = Ep(nc.sbuf_tensor("ghh", [128, 64 * 1024], BF16))
        owsl = Ep(nc.sbuf_tensor("owsl", [128, 64 * 256], BF16))
        aostg = Ep(nc.sbuf_tensor("aostg", [128, 4 * 256], F16))
        for half in range(2):
            def _ldg(g, half=half):
                src = bass.AP(gh_s.tensor, half * 1024,
                              [[TPC, 128], [128 * TPC, 64], [1, 1024]])
                dst = bass.AP(ghh, 0, [[64 * 1024, 128], [1024, 64], [1, 1024]])
                P.dma(g, dst, src, True)
            P.stage(gpsimd=_ldg)
            for osl in range(16):
                def _ldo(g, osl=osl):
                    src = bass.AP(out_wT.tensor, osl * 256,
                                  [[HID, 128], [128 * HID, 64], [1, 256]])
                    dst = bass.AP(owsl, 0, [[64 * 256, 128], [256, 64], [1, 256]])
                    P.dma(g, dst, src, True)
                P.stage(gpsimd=_ldo)
                for tq in range(2):
                    def _om(t, tq=tq):
                        for u in range(4):
                            tt8 = tq * 4 + u
                            for ct in range(64):
                                i = t.matmul(SLOT6[u][:, 0:256],
                                             ghh[:, ct * 1024 + tt8 * 128:ct * 1024 + tt8 * 128 + 128],
                                             owsl[:, ct * 256:(ct + 1) * 256],
                                             start=(ct == 0), stop=(ct == 63))
                        return i
                    P.stage(tensor=_om)

                    def _oe(s, tq=tq):
                        for u in range(4):
                            i = s.activation(aostg[:, u * 256:(u + 1) * 256],
                                             SLOT6[u][:, 0:256], AF.Copy)
                        return i
                    P.stage(scalar=_oe)

                    def _ost(sy, half=half, osl=osl, tq=tq):
                        r0 = half * 1024 + tq * 512
                        dst = bass.AP(ao_s.tensor, r0 * HID + osl * 256,
                                      [[HID, 128], [128 * HID, 4], [1, 256]])
                        src = bass.AP(aostg, 0, [[1024, 128], [256, 4], [1, 256]])
                        P.dma(sy, dst, src, False)
                    P.stage(sync=_ost)

    with ExitStack() as ph:
        Ep = ph.enter_context
        aorow = Ep(nc.sbuf_tensor("aorow", [128, HID], F16))
        xrow = Ep(nc.sbuf_tensor("xrow", [128, HID], F32))
        vrow = Ep(nc.sbuf_tensor("vrow", [128, HID], F32))
        stat = Ep(nc.sbuf_tensor("stat", [128, 48], F32))
        mv = Ep(nc.sbuf_tensor("mv", [128, 2], F32))
        nmean = Ep(nc.sbuf_tensor("nmean", [128, 1], F32))
        rstd = Ep(nc.sbuf_tensor("rstd", [128, 1], F32))
        x2row = Ep(nc.sbuf_tensor("x2row", [128, HID], F32))
        x2b = Ep(nc.sbuf_tensor("x2b", [128, HID], BF16))
        xtstg = Ep(nc.sbuf_tensor("xtstg", [128, 8 * 128], BF16))
        for tt in range(16):
            def _ldr(g, tt=tt):
                P.dma(g, aorow[:], ao_s[tt * 128:(tt + 1) * 128, :], True)
                P.dma(g, xrow[:], x_res[tt * 128:(tt + 1) * 128, :], True)
            P.stage(gpsimd=_ldr)
            P.stage(vector=lambda v: v.tensor_add(vrow[:], xrow[:], aorow[:]))

            def _bn(v):
                for u in range(8):
                    i = v.bn_stats(stat[:, u * 6:(u + 1) * 6], vrow[:, u * 512:(u + 1) * 512])
                return i
            P.stage(vector=_bn)
            P.stage(vector=lambda v: v.bn_aggr(mv[:], stat[:]))
            P.stage(vector=lambda v: v.tensor_scalar(nmean[:], mv[:, 0:1], -1.0, None, OP.mult))
            P.stage(scalar=lambda s: s.activation(rstd[:], mv[:, 1:2], AF.Ln, bias=constp[:, 0:1]))
            P.stage(scalar=lambda s: s.activation(rstd[:], rstd[:], AF.Exp, scale=constp[:, 2:3]))
            P.stage(vector=lambda v: v.tensor_scalar(x2row[:], vrow[:], nmean[:, 0:1],
                                                     rstd[:, 0:1], OP.add, OP.mult))
            P.stage(vector=lambda v: v.tensor_copy(x2b[:], x2row[:]))
            P.stage(sync=lambda sy, tt=tt: P.dma(sy, x2_s[tt * 128:(tt + 1) * 128, :], x2row[:], False))
            for r in range(4):
                def _xt(t, r=r):
                    for u in range(8):
                        k = r * 8 + u
                        i = t.transpose(pbt[:, u * 128:(u + 1) * 128],
                                        x2b[:, k * 128:(k + 1) * 128], ident_bf[:])
                    return i
                P.stage(tensor=_xt)

                def _xe(v):
                    for u in range(8):
                        i = v.tensor_copy(xtstg[:, u * 128:(u + 1) * 128],
                                          pbt[:, u * 128:(u + 1) * 128])
                    return i
                P.stage(vector=_xe)

                def _xst(sy, tt=tt, r=r):
                    dst = bass.AP(x2T_s.tensor, (r * 8) * 128 * TPC + tt * 128,
                                  [[TPC, 128], [128 * TPC, 8], [1, 128]])
                    src = bass.AP(xtstg, 0, [[1024, 128], [128, 8], [1, 128]])
                    P.dma(sy, dst, src, False)
                P.stage(sync=_xst)

    with ExitStack() as ph:
        Ep = ph.enter_context
        qt = Ep(nc.sbuf_tensor("qt", [128, DFF], U8))
        qn = Ep(nc.sbuf_tensor("qn", [128, DFF], U8))
        zt = Ep(nc.sbuf_tensor("zt", [128, DFF // 2], U8))
        zn = Ep(nc.sbuf_tensor("zn", [128, DFF], U8))
        sbt = Ep(nc.sbuf_tensor("sbt", [128, DFF], F16))
        tb = Ep(nc.sbuf_tensor("tb", [128, DFF], BF16))
        wstg = Ep(nc.sbuf_tensor("wstg", [128, DFF], BF16))

        def emit_deq(qsrc, zsrc, ssrc, wdst, ntile, width):
            for ft in range(ntile):
                def _ld(g, ft=ft):
                    for e in range(2):
                        for b in range(4):
                            src = bass.AP(qsrc.tensor, (64 * ft + b) * width,
                                          [[4 * width, 16], [1, width]])
                            dst = bass.AP(qt, (e * 64 + 16 * b) * DFF,
                                          [[DFF, 16], [1, width]])
                            P.dma(g, dst, src, True)
                    P.dma(g, zt[:, 0:width // 2],
                          zsrc[ft * 128:(ft + 1) * 128, :], True)
                    P.dma(g, sbt[:, 0:width],
                          ssrc[ft * 128:(ft + 1) * 128, :], True)
                P.stage(gpsimd=_ld)

                def _un(v, width=width):
                    v.tensor_scalar(qn[0:64, 0:width], qt[0:64, 0:width], 15, None, OP.bitwise_and)
                    v.tensor_scalar(qn[64:128, 0:width], qt[64:128, 0:width], 4, 15,
                                    OP.logical_shift_right, OP.bitwise_and)
                    zv = bass.AP(zn, 0, [[DFF, 128], [2, width // 2]])
                    zv2 = bass.AP(zn, 1, [[DFF, 128], [2, width // 2]])
                    v.tensor_scalar(zv, zt[:, 0:width // 2], 15, None, OP.bitwise_and)
                    return v.tensor_scalar(zv2, zt[:, 0:width // 2], 4, 15,
                                           OP.logical_shift_right, OP.bitwise_and)
                P.stage(vector=_un)
                P.stage(vector=lambda v, width=width: v.tensor_sub(
                    tb[:, 0:width], qn[:, 0:width], zn[:, 0:width]))
                P.stage(vector=lambda v, width=width: v.scalar_tensor_tensor(
                    wstg[:, 0:width], tb[:, 0:width], -1.0, sbt[:, 0:width],
                    OP.add, OP.mult))
                P.stage(sync=lambda sy, ft=ft, width=width: P.dma(
                    sy, wdst[ft * 128:(ft + 1) * 128, :], wstg[:, 0:width], False))

        emit_deq(qg, zgb, sgb, wg_s, 32, DFF)
        emit_deq(qu, zub, sub_, wu_s, 32, DFF)
        emit_deq(qd, zdb, sdb, wd_s, 86, HID)

    with ExitStack() as ph:
        Ep = ph.enter_context
        wgs = Ep(nc.sbuf_tensor("wgs", [128, 32 * 512], BF16))
        wus = Ep(nc.sbuf_tensor("wus", [128, 32 * 512], BF16))
        x2s = Ep(nc.sbuf_tensor("x2s", [128, 32 * 512], BF16))
        sgt = Ep(nc.sbuf_tensor("sgt", [128, 512], BF16))
        itstg = Ep(nc.sbuf_tensor("itstg", [128, 4 * 512], BF16))
        NSL = 22
        for sl in range(NSL):
            n0 = sl * 512
            nw = min(512, DFF - n0)

            def _lw(g, n0=n0, nw=nw):
                src = bass.AP(wg_s.tensor, n0, [[DFF, 128], [128 * DFF, 32], [1, nw]])
                dst = bass.AP(wgs, 0, [[32 * 512, 128], [512, 32], [1, nw]])
                P.dma(g, dst, src, True)
                src = bass.AP(wu_s.tensor, n0, [[DFF, 128], [128 * DFF, 32], [1, nw]])
                dst = bass.AP(wus, 0, [[32 * 512, 128], [512, 32], [1, nw]])
                P.dma(g, dst, src, True)
            P.stage(gpsimd=_lw)
            for tq in range(4):
                def _lx(g, tq=tq):
                    for e in range(2):
                        for b in range(4):
                            src = bass.AP(x2T_s.tensor, (e + 2 * b) * TPC + tq * 512,
                                          [[8 * TPC, 16], [128 * TPC, 32], [1, 512]])
                            dst = bass.AP(x2s, (e * 64 + 16 * b) * 32 * 512,
                                          [[32 * 512, 16], [512, 32], [1, 512]])
                            P.dma(g, dst, src, True)
                P.stage(gpsimd=_lx)
                ntl = (nw + 127) // 128
                for ni in range(ntl):
                    cw = min(128, nw - ni * 128)

                    def _gm(t, ni=ni, cw=cw):
                        for k in range(32):
                            t.matmul(psA[0:cw, 0:512],
                                     wgs[:, k * 512 + ni * 128:k * 512 + ni * 128 + cw],
                                     x2s[:, k * 512:(k + 1) * 512],
                                     start=(k == 0), stop=(k == 31))
                        for k in range(32):
                            i = t.matmul(psA[0:cw, 512:1024],
                                         wus[:, k * 512 + ni * 128:k * 512 + ni * 128 + cw],
                                         x2s[:, k * 512:(k + 1) * 512],
                                         start=(k == 0), stop=(k == 31))
                        return i
                    P.stage(tensor=_gm)
                    P.stage(scalar=lambda s, cw=cw: s.activation(
                        sgt[0:cw, :], psA[0:cw, 0:512], AF.Silu))
                    P.stage(vector=lambda v, ni=ni, cw=cw: v.tensor_mul(
                        itstg[0:cw, ni * 512:ni * 512 + 512], sgt[0:cw, :], psA[0:cw, 512:1024]))

                def _ist(sy, n0=n0, nw=nw, tq=tq, ntl=ntl):
                    dst = bass.AP(interT_s.tensor, n0 * TPC + tq * 512,
                                  [[TPC, 128], [128 * TPC, ntl], [1, 512]])
                    src = bass.AP(itstg, 0, [[2048, 128], [512, ntl], [1, 512]])
                    P.dma(sy, dst, src, False)
                P.stage(sync=_ist)

    with ExitStack() as ph:
        Ep = ph.enter_context
        its = Ep(nc.sbuf_tensor("its", [128, 86 * 512], BF16))
        wds = Ep(nc.sbuf_tensor("wds", [128, 86 * 256], BF16))
        ffe = Ep(nc.sbuf_tensor("ffe", [128, 512], F16))
        ffo = Ep(nc.sbuf_tensor("ffo", [128, 4 * HID], F16))
        x2r = Ep(nc.sbuf_tensor("x2r", [128, HID], F32))
        v2 = Ep(nc.sbuf_tensor("v2", [128, HID], F32))
        stat2 = Ep(nc.sbuf_tensor("stat2", [128, 48], F32))
        mv2 = Ep(nc.sbuf_tensor("mv2", [128, 2], F32))
        nm2 = Ep(nc.sbuf_tensor("nm2", [128, 1], F32))
        rs2 = Ep(nc.sbuf_tensor("rs2", [128, 1], F32))
        for tq in range(4):
            def _li(g, tq=tq):
                for e in range(2):
                    for b in range(4):
                        src = bass.AP(interT_s.tensor, (e + 2 * b) * TPC + tq * 512,
                                      [[8 * TPC, 16], [128 * TPC, 86], [1, 512]])
                        dst = bass.AP(its, (e * 64 + 16 * b) * 86 * 512,
                                      [[86 * 512, 16], [512, 86], [1, 512]])
                        P.dma(g, dst, src, True)
            P.stage(gpsimd=_li)
            for dsl in range(16):
                def _lwd(g, dsl=dsl):
                    src = bass.AP(wd_s.tensor, dsl * 256, [[HID, 128], [128 * HID, 86], [1, 256]])
                    dst = bass.AP(wds, 0, [[86 * 256, 128], [256, 86], [1, 256]])
                    P.dma(g, dst, src, True)
                P.stage(gpsimd=_lwd)
                for du in range(2):
                    def _dm(t, du=du):
                        for k in range(86):
                            i = t.matmul(SLOT6[du][:, 0:512],
                                         wds[:, k * 256 + du * 128:k * 256 + du * 128 + 128],
                                         its[:, k * 512:(k + 1) * 512],
                                         start=(k == 0), stop=(k == 85))
                        return i
                    P.stage(tensor=_dm)
                    P.stage(scalar=lambda s, du=du: s.activation(ffe[:], SLOT6[du][:, 0:512], AF.Copy))

                    def _dt(t):
                        for u in range(4):
                            i = t.transpose(pbt2[:, u * 128:(u + 1) * 128],
                                            ffe[:, u * 128:(u + 1) * 128], ident_h[:])
                        return i
                    P.stage(tensor=_dt)

                    def _de(v, dsl=dsl, du=du):
                        d0 = dsl * 256 + du * 128
                        for u in range(4):
                            i = v.tensor_copy(ffo[:, u * HID + d0:u * HID + d0 + 128],
                                              pbt2[:, u * 128:(u + 1) * 128])
                        return i
                    P.stage(vector=_de)
            for u in range(4):
                tt = tq * 4 + u

                def _lx2(g, tt=tt):
                    P.dma(g, x2r[:], x2_s[tt * 128:(tt + 1) * 128, :], True)
                P.stage(gpsimd=_lx2)
                P.stage(vector=lambda v, u=u: v.tensor_add(v2[:], x2r[:], ffo[:, u * HID:(u + 1) * HID]))

                def _bn2(v):
                    for w in range(8):
                        i = v.bn_stats(stat2[:, w * 6:(w + 1) * 6], v2[:, w * 512:(w + 1) * 512])
                    return i
                P.stage(vector=_bn2)
                P.stage(vector=lambda v: v.bn_aggr(mv2[:], stat2[:]))
                P.stage(vector=lambda v: v.tensor_scalar(nm2[:], mv2[:, 0:1], -1.0, None, OP.mult))
                P.stage(scalar=lambda s: s.activation(rs2[:], mv2[:, 1:2], AF.Ln, bias=constp[:, 0:1]))
                P.stage(scalar=lambda s: s.activation(rs2[:], rs2[:], AF.Exp, scale=constp[:, 2:3]))
                P.stage(vector=lambda v: v.tensor_scalar(x2r[:], v2[:], nm2[:, 0:1],
                                                         rs2[:, 0:1], OP.add, OP.mult))
                P.stage(sync=lambda sy, tt=tt: P.dma(sy, y_out[tt * 128:(tt + 1) * 128, :], x2r[:], False))


def _host_prep(inputs):
    x = np.asarray(inputs["L0_attn_input_states"])
    w_in = np.asarray(inputs["L0_attn_in_proj_w"])
    conv_w = np.asarray(inputs["L0_attn_conv_w"])[:, 0, :]
    conv_b = np.asarray(inputs["L0_attn_conv_b"])
    dt_bias = np.asarray(inputs["L0_attn_dt_bias"])
    A_log = np.asarray(inputs["L0_attn_A_log"])
    Dp = np.asarray(inputs["L0_attn_D"])
    norm_w = np.asarray(inputs["L0_attn_norm_w"])
    out_w = np.asarray(inputs["L0_attn_out_w"])

    com = {}
    com["w_inT"] = np.ascontiguousarray(w_in.T)
    com["out_wT"] = np.ascontiguousarray(out_w.T)
    com["convw"] = np.ascontiguousarray(conv_w.reshape(80, 128, 4).transpose(1, 0, 2).reshape(128, 320))
    com["convb"] = np.ascontiguousarray(conv_b.reshape(80, 128).T)
    com["dtb_col"] = np.ascontiguousarray(dt_bias.reshape(128, 1))
    com["alog_col"] = np.ascontiguousarray(A_log.reshape(128, 1))
    com["onesr"] = np.ones((1, 128), np.float32)
    com["onescol"] = np.ones((128, 1), np.float32)
    com["identf"] = np.eye(128, dtype=np.float32)
    com["d_bc"] = np.ascontiguousarray(np.broadcast_to(Dp[None, :], (128, 128)))
    com["normw"] = np.ascontiguousarray(norm_w.reshape(64, 128).T)
    m = (np.arange(CS)[None, :] >= np.arange(CS)[:, None]).astype(np.float32)
    com["masku"] = np.ascontiguousarray(
        np.concatenate([m[0:128, :], m[128:256, :]], axis=1)).astype(np.uint8)

    def qprep(qw):
        return np.ascontiguousarray(
            qw.view(np.uint8).reshape(qw.shape[0], qw.shape[1], 4)
            .transpose(0, 2, 1).reshape(qw.shape[0] * 4, qw.shape[1]))

    com["qg"] = qprep(np.asarray(inputs["L0_ffn_gate_qweight"]))
    com["qu"] = qprep(np.asarray(inputs["L0_ffn_up_qweight"]))
    com["qd"] = qprep(np.asarray(inputs["L0_ffn_down_qweight"]))

    def sprep(s, reps):
        return np.ascontiguousarray(
            np.broadcast_to(s[:, None, :], (s.shape[0], 128, s.shape[1]))
            .reshape(s.shape[0] * 128, s.shape[1]))[0:reps * 128]

    com["sgb"] = sprep(np.asarray(inputs["L0_ffn_gate_scales"]), 32)
    com["sub_"] = sprep(np.asarray(inputs["L0_ffn_up_scales"]), 32)
    com["sdb"] = sprep(np.asarray(inputs["L0_ffn_down_scales"]), 86)[0:DFF]

    def zprep(z, reps, width):
        zb = z.view(np.uint8)
        return np.ascontiguousarray(
            np.broadcast_to(zb[:, None, :], (zb.shape[0], 128, zb.shape[1]))
            .reshape(zb.shape[0] * 128, zb.shape[1]))[0:reps * 128]

    com["zgb"] = zprep(np.asarray(inputs["L0_ffn_gate_zeros"]), 32, DFF)
    com["zub"] = zprep(np.asarray(inputs["L0_ffn_up_zeros"]), 32, DFF)
    com["zdb"] = zprep(np.asarray(inputs["L0_ffn_down_zeros"]), 86, HID)[0:DFF]

    percore = []
    for c in range(8):
        b, s = c // 4, c % 4
        t0 = 2048 * s
        seg = x[b, t0:t0 + TPC]
        xTc = np.zeros((TP4, HID), np.float32)
        if s > 0:
            xTc[1:4] = x[b, t0 - 3:t0]
        xTc[4:] = seg
        d = dict(xT=np.ascontiguousarray(xTc.T), x_res=np.ascontiguousarray(seg))
        msel = np.zeros(8, np.float32)
        mpd = np.zeros((8, 8), np.float32)
        for j in range(8):
            if j // 4 == b and j % 4 < s:
                msel[j] = 1.0
                for k in range(8):
                    if k // 4 == b and (j % 4) < (k % 4) < s:
                        mpd[j, k] = 1.0
        d["mselp"] = msel.reshape(1, 8)
        d["mpd_m"] = mpd.reshape(1, 64).astype(np.float32)
        d["mpd_1m"] = (1.0 - mpd).reshape(1, 64).astype(np.float32)
        percore.append(d)
    return com, percore


def kernel(**inputs):
    com, percore = _host_prep(inputs)
    l1_keys = ["xT", "w_inT", "convw", "convb", "dtb_col", "alog_col",
               "onesr", "onescol", "identf"]
    nc1 = build_nc("L1")
    maps1 = []
    for c in range(8):
        m = {k: com[k] for k in l1_keys if k in com}
        m.update({k: percore[c][k] for k in ("xT",)})
        maps1.append(m)
    r1 = run_bass_kernel_spmd(nc1, maps1, core_ids=list(range(8)))
    Lstack = np.concatenate([r1.results[c]["L_out"] for c in range(8)], axis=0)
    Dflat = np.concatenate([r1.results[c]["D_out"][0] for c in range(8)]).reshape(1, 8 * NH)

    nc2 = build_nc("L2")
    maps2 = []
    for c in range(8):
        m = dict(com)
        m.update(percore[c])
        m["Lstack"] = Lstack
        m["Dflat"] = Dflat
        maps2.append(m)
    r2 = run_bass_kernel_spmd(nc2, maps2, core_ids=list(range(8)))
    out = np.zeros((2, 8192, HID), np.float32)
    for c in range(8):
        b, s = c // 4, c % 4
        out[b, 2048 * s:2048 * s + TPC] = r2.results[c]["y_out"]
    return out
